# version 02
# baseline (speedup 1.0000x reference)
from contextlib import ExitStack

import numpy as np
import concourse.bass as bass
import concourse.mybir as mybir

F32 = mybir.dt.float32
BF16 = mybir.dt.bfloat16
AF = mybir.ActivationFunctionType
ALU = mybir.AluOpType
AX = mybir.AxisListType

ENGINES = ("tensor", "vector", "scalar", "gpsimd", "sync")
SELF_GUARD = {"vector", "scalar", "gpsimd"}
PROG = {"tensor": "T", "vector": "V", "scalar": "A", "gpsimd": "G"}


class Pl:
    def __init__(self, nc, stack):
        self.nc = nc
        self.stack = stack
        self.ops = {e: [] for e in ENGINES}
        self.cnt = {}
        self.sems = {}
        self.maxwait = {}
        self.issuer_wait = {}
        self.t_pos = []
        self.t_demanded = set()

    def sem(self, name):
        if name not in self.sems:
            self.sems[name] = self.stack.enter_context(self.nc.semaphore(name))
            self.cnt[name] = 0
        return self.sems[name]

    DMA_SEMS = ("dL", "dS")

    def wait(self, eng, sem_name, val):
        if val <= 0:
            return
        assert val <= self.cnt.get(sem_name, 0), (
            f"wait {eng} {sem_name}>={val} but planned count is {self.cnt.get(sem_name, 0)}")
        if sem_name in self.DMA_SEMS:
            val = self.cnt[sem_name]
        self.maxwait[sem_name] = max(self.maxwait.get(sem_name, 0), val)
        if val > self.issuer_wait.get((eng, sem_name), 0):
            self.issuer_wait[eng, sem_name] = val
        if sem_name == "T":
            self.t_demanded.add(val)
            self.ops[eng].append(("wT", self.sems[sem_name], val))
            return
        self.ops[eng].append(("w", self.sems[sem_name], val))

    def do(self, eng, fn, inc=None, n=1, guard=True):
        if eng in SELF_GUARD and guard:
            own = PROG[eng]
            self.wait(eng, own, self.cnt[own])
            if inc is None:
                inc = own
        if inc == "T":
            self.sem("T")
            self.cnt["T"] += 1
            self.t_pos.append(len(self.ops["tensor"]))
            self.ops["tensor"].append(("x", fn, []))
            return self.cnt["T"]
        incs = []
        if inc is not None:
            h = self.sem(inc)
            self.cnt[inc] += n
            incs = [(h, n)]
        self.ops[eng].append(("x", fn, incs))
        return self.cnt[inc] if inc is not None else None

    def dma(self, eng, out, in_, inc, **kw):
        mw = self.maxwait.get(inc, 0)
        if mw > self.issuer_wait.get((eng, inc), 0):
            self.wait(eng, inc, mw)
        def fn(e, out=out, in_=in_, kw=kw):
            return e.dma_start(out=out, in_=in_, **kw)
        return self.do(eng, fn, inc=inc, n=16, guard=False)

    def barrier(self):
        snap = dict(self.cnt)
        for e in ENGINES:
            for s, v in snap.items():
                self.wait(e, s, v)

    def replay(self, block):
        milestones = sorted(v for v in self.t_demanded if v > 0)
        t_rank = {v: i + 1 for i, v in enumerate(milestones)}
        h_t = self.sems.get("T")
        for v in milestones:
            pos = self.t_pos[v - 1]
            op = self.ops["tensor"][pos]
            op[2].append((h_t, 1))
        def mk(eng_name):
            def body(eng):
                for op in self.ops[eng_name]:
                    if op[0] == "w":
                        eng.wait_ge(op[1], op[2])
                    elif op[0] == "wT":
                        eng.wait_ge(op[1], t_rank[op[2]])
                    else:
                        ins = op[1](eng)
                        for (h, n) in op[2]:
                            ins.then_inc(h, n)
            return body
        for e in ENGINES:
            if self.ops[e]:
                getattr(block, e)(mk(e))


def ceil_div(a, b):
    return -(-a // b)


def build_program(cfg, dbg=()):
    S = cfg["S"]; D = cfg["D"]; H = cfg["H"]; B = cfg["B"]
    HD = 128
    assert D == H * HD
    C = D // 128
    TT = S // 128
    QC = min(512, S)
    NQC = S // QC
    JPQ = QC // 128
    NSL = max(1, S // 512)
    SLW = min(512, S)
    SC_SOFT = float(1.0 / np.float32(np.sqrt(np.float32(HD))))
    TH_X, SC_X = 0.33, 0.5
    TH_W, SC_W = 0.2, 0.6
    EPS = 1e-5

    nc = bass.Bass("TRN2", target_bir_lowering=False)
    stack = ExitStack()

    x_in = nc.dram_tensor("x", [B, S, D], F32, kind="ExternalInput")
    w_in = {nm: nc.dram_tensor(nm, [D, D], F32, kind="ExternalInput")
            for nm in ("wq", "wk", "wv", "wo", "wf", "rr")}
    bo_in = nc.dram_tensor("bo", [1, D], F32, kind="ExternalInput")
    id_in = nc.dram_tensor("identf", [128, 128], F32, kind="ExternalInput")
    cm_in = nc.dram_tensor("cmask", [128, 128], F32, kind="ExternalInput")
    y_out = nc.dram_tensor("y", [B, S, D], F32, kind="ExternalOutput")

    wspl = {}
    for nm in ("wq", "wk", "wv", "wo", "wfT"):
        for p in ("hi", "lo"):
            wspl[nm, p] = nc.dram_tensor(f"{nm}_{p}", [C, 128, C, 128], BF16)
    qs = {p: nc.dram_tensor(f"qs_{p}", [D, S], BF16) for p in ("hi", "lo")}
    ks = {p: nc.dram_tensor(f"ks_{p}", [D, S], BF16) for p in ("hi", "lo")}
    vs = {p: nc.dram_tensor(f"vs_{p}", [H, S, HD], BF16) for p in ("hi", "lo")}
    x2s = nc.dram_tensor("x2s", [S, D], F32)

    dbg_t = {}
    def dbgt(name, shape):
        dbg_t[name] = nc.dram_tensor("dbg_" + name, shape, F32, kind="ExternalOutput")
    if "hT" in dbg:
        dbgt("hT_hi", [D, S]); dbgt("hT_lo", [D, S])
    if "qkv" in dbg:
        dbgt("q", [D, S]); dbgt("k", [D, S]); dbgt("v", [H, S, HD])
    if "ctx" in dbg:
        dbgt("ctx", [D, S])
    if "x2" in dbg:
        dbgt("x2", [S, D])
    if "wr" in dbg:
        dbgt("wrT", [D, D])

    def mksb(st_, name, shape, dt=F32):
        return st_.enter_context(nc.sbuf_tensor(name, shape, dt))

    MAXW = max(S, D)
    pool = [mksb(stack, f"pf{i}", [128, MAXW], F32) for i in range(4)]
    bpool = [mksb(stack, f"pb{i}", [128, D], BF16) for i in range(2)]
    wtile = [mksb(stack, f"pw{i}", [128, C * 128], BF16) for i in range(2)]
    ev4 = [mksb(stack, f"pe{i}", [128, 512], BF16) for i in range(4)]
    kq4 = [mksb(stack, f"kq{i}", [128, S], BF16) for i in range(4)]
    vh2 = [mksb(stack, f"vh{i}", [128, TT * 129], BF16) for i in range(2)]
    identf = mksb(stack, "identf_sb", [128, 128], F32)
    identb = mksb(stack, "identb_sb", [128, 128], BF16)
    cmask = mksb(stack, "cmask_sb", [128, 128], F32)
    negC = mksb(stack, "negC", [128, 1], F32)
    bo_sb = mksb(stack, "bo_sb", [128, C], F32)
    stat = mksb(stack, "stat", [128, 8], F32)
    lrec = mksb(stack, "lrec", [128, 4], F32)
    slab = mksb(stack, "slab", [128, 2 * C * S], BF16)
    hT = {"hi": slab[:, 0:C * S], "lo": slab[:, C * S:2 * C * S]}
    cxT = hT
    xqT = slab[:, 0:C * S]

    pQ = [stack.enter_context(nc.psum_tensor(f"pQ{i}", [128, 512], F32)) for i in range(4)]
    pTf = pQ[3]
    pSC = stack.enter_context(nc.psum_tensor("pSC", [128, QC], F32))
    pCXa = stack.enter_context(nc.psum_tensor("pCXa", [128, 258], F32))
    pCXc = stack.enter_context(nc.psum_tensor("pCXb", [128, 258], F32))
    pCX = [pCXa[:, 0:129], pCXa[:, 129:258], pCXc[:, 0:129], pCXc[:, 129:258]]
    pTb = stack.enter_context(nc.psum_tensor("pTb", [128, 1024], BF16))
    pT1 = pTb[:, 0:512]
    pT2 = pTb[:, 512:1024]

    P = Pl(nc, stack)
    for s in ("T", "V", "A", "G", "dL", "dS"):
        P.sem(s)

    def t_do(fn):
        return P.do("tensor", fn, inc="T")

    def v_do(fn):
        return P.do("vector", fn)

    def a_do(fn):
        return P.do("scalar", fn)

    def ld(out, in_, eng="sync"):
        return P.dma(eng, out, in_, "dL")

    def st(out, in_, eng="gpsimd"):
        return P.dma(eng, out, in_, "dS")

    def load_wtile(dst, nm, p, ot):
        return ld(dst[:, 0:C * 128], wspl[nm, p].ap()[ot].rearrange("p c n -> p (c n)"))

    def store_wtile_block(nm, p, src, c):
        dest = wspl[nm, p].ap().rearrange("t p c n -> p t c n")[:, :, c, :]
        for t4 in range(ceil_div(C, 4)):
            n4 = min(4, C - t4 * 4)
            st(dest[:, t4 * 4:t4 * 4 + n4, :],
               src[:, t4 * 4 * 128:(t4 * 4 + n4) * 128].rearrange("p (t n) -> p t n", n=128))

    c1 = ld(identf[:], id_in.ap())
    c2 = ld(cmask[:], cm_in.ap())
    c3 = None
    for c in range(C):
        c3 = ld(bo_sb[:, c:c + 1], bo_in.ap()[0:1, c * 128:(c + 1) * 128].rearrange("a (p n) -> (a p) n", n=1))
    P.wait("vector", "dL", c3)
    v_do(lambda v: v.tensor_copy(identb[:], identf[:]))
    v_do(lambda v: v.memset(negC[:], -12.0))

    if not cfg.get("skip_pre"):
        wld = [pool[0][:, 0:D], pool[0][:, 0:D]]
        rrl = pool[1][:, 0:D]
        whib = bpool[0][:]
        wlob = bpool[1][:]
        wg1 = pool[2][:, 0:D]
        wg2 = bpool[0][:, 0:D]
        wrf = pool[3][:, 0:D]
        wtev = {"hi": ev4[0][:], "lo": ev4[1][:]}

        for nm in ("wq", "wk", "wv", "wo"):
            for c in range(C):
                buf = wld[c % 2]
                ds0 = P.cnt["dS"]
                P.wait("sync", "dS", ds0)
                P.wait("sync", "A", P.cnt["A"])
                P.wait("sync", "V", P.cnt["V"])
                d = ld(buf, w_in[nm].ap()[c * 128:(c + 1) * 128, :])
                P.wait("scalar", "dL", d)
                P.wait("scalar", "dS", ds0)
                ah = a_do(lambda s, b=buf: s.copy(whib, b))
                P.wait("vector", "dL", d)
                P.wait("vector", "dS", ds0)
                P.wait("vector", "A", ah)
                vl = v_do(lambda v, b=buf: v.tensor_sub(wlob, b, whib))
                P.wait("gpsimd", "A", ah)
                P.wait("gpsimd", "V", vl)
                store_wtile_block(nm, "hi", whib, c)
                store_wtile_block(nm, "lo", wlob, c)

        for r in range(C):
            buf = wld[r % 2]
            ds0 = P.cnt["dS"]
            P.wait("sync", "dS", ds0)
            P.wait("sync", "A", P.cnt["A"])
            P.wait("sync", "V", P.cnt["V"])
            d1 = ld(buf, w_in["wf"].ap()[r * 128:(r + 1) * 128, :])
            d2 = ld(rrl, w_in["rr"].ap()[r * 128:(r + 1) * 128, :])
            P.wait("vector", "dL", d2)
            v_do(lambda v, b=buf: v.tensor_single_scalar(wg1, b, TH_W, ALU.is_gt))
            v_do(lambda v, b=buf: v.tensor_single_scalar(wg2, b, -TH_W, ALU.is_lt))
            v_do(lambda v: v.tensor_sub(wg1, wg1, wg2))
            vw = v_do(lambda v: v.scalar_tensor_tensor(wrf, wg1, SC_W, rrl, ALU.mult, ALU.add))
            for c4 in range(ceil_div(C, 4)):
                n4 = min(4, C - c4 * 4)
                P.wait("tensor", "V", vw)
                P.wait("tensor", "A", P.cnt["A"])
                P.wait("tensor", "V", P.cnt["V"])
                tt_ = None
                for k in range(n4):
                    cc = c4 * 4 + k
                    tt_ = t_do(lambda t, cc=cc, k=k: t.transpose(
                        pTf[:, k * 128:(k + 1) * 128], wrf[:, cc * 128:(cc + 1) * 128], identf[:]))
                ds0 = P.cnt["dS"]
                P.wait("scalar", "T", tt_)
                P.wait("scalar", "dS", ds0)
                ae = a_do(lambda s, n4=n4: s.copy(wtev["hi"][:, 0:n4 * 128], pTf[:, 0:n4 * 128]))
                P.wait("vector", "T", tt_)
                P.wait("vector", "dS", ds0)
                P.wait("vector", "A", ae)
                ve = v_do(lambda v, n4=n4: v.tensor_sub(
                    wtev["lo"][:, 0:n4 * 128], pTf[:, 0:n4 * 128], wtev["hi"][:, 0:n4 * 128]))
                P.wait("gpsimd", "A", ae)
                P.wait("gpsimd", "V", ve)
                for k in range(n4):
                    cc = c4 * 4 + k
                    dest = wspl["wfT", "hi"].ap().rearrange("t p c n -> p t c n")[:, r, cc, :]
                    st(dest, wtev["hi"][:, k * 128:(k + 1) * 128])
                    dest = wspl["wfT", "lo"].ap().rearrange("t p c n -> p t c n")[:, r, cc, :]
                    st(dest, wtev["lo"][:, k * 128:(k + 1) * 128])
        if "wr" in dbg:
            P.barrier()
            for c in range(C):
                dd = ld(whib, wspl["wfT", "hi"].ap()[c].rearrange("p c n -> p (c n)"))
                dd2 = ld(wlob, wspl["wfT", "lo"].ap()[c].rearrange("p c n -> p (c n)"))
                P.wait("vector", "dL", dd2)
                vv = v_do(lambda v: v.tensor_add(wrf, whib, wlob))
                P.wait("gpsimd", "V", vv)
                st(dbg_t["wrT"].ap()[c * 128:(c + 1) * 128, :], wrf)
                P.barrier()
        P.barrier()
    P.barrier()

    STOP = cfg.get("stop", 99)
    for b in range(B if STOP >= 1 else 0):
        if True:
            xtile = [pool[0][:, 0:D], pool[0][:, 0:D]]
            sqt = pool[1][:, 0:D]
            htmp = pool[1]
            hhib = bpool[0]
            hlob = bpool[1]
            whead = [{"hi": wtile[0], "lo": wtile[1]} for i in range(2)]
            evhi = ev4[0]
            evlo = ev4[1]
            vtr = {"hi": ev4[2], "lo": ev4[3]}

            for tt in range(TT):
                xb = xtile[tt % 2]
                P.wait("sync", "A", P.cnt["A"])
                P.wait("sync", "V", P.cnt["V"])
                dx = ld(xb, x_in.ap()[b, tt * 128:(tt + 1) * 128, :])
                P.wait("vector", "dL", dx)
                v_do(lambda v, xb=xb: v.reduce_sum(stat[:, 0:1], xb, axis=AX.X))
                P.wait("scalar", "dL", dx)
                sq = a_do(lambda s, xb=xb: s.activation(sqt, xb, AF.Square, accum_out=stat[:, 1:2]))
                v_do(lambda v: v.tensor_scalar_mul(stat[:, 2:3], stat[:, 0:1], 1.0 / D))
                P.wait("vector", "A", sq)
                v_do(lambda v: v.tensor_mul(stat[:, 3:4], stat[:, 2:3], stat[:, 2:3]))
                v_do(lambda v: v.tensor_scalar(stat[:, 4:5], stat[:, 1:2], 1.0 / D, None, ALU.mult))
                v_do(lambda v: v.tensor_sub(stat[:, 4:5], stat[:, 4:5], stat[:, 3:4]))
                v_do(lambda v: v.tensor_scalar_add(stat[:, 4:5], stat[:, 4:5], EPS))
                vr = v_do(lambda v: v.reciprocal(stat[:, 5:6], stat[:, 4:5]))
                P.wait("scalar", "V", vr)
                ar = a_do(lambda s: s.activation(stat[:, 6:7], stat[:, 5:6], AF.Sqrt))
                P.wait("vector", "A", ar)
                vb2 = v_do(lambda v: v.scalar_tensor_tensor(stat[:, 7:8], stat[:, 2:3], -1.0, stat[:, 6:7], ALU.mult, ALU.mult))
                vhi = v_do(lambda v, xb=xb: v.tensor_scalar(hhib[:], xb, stat[:, 2:3], stat[:, 6:7], ALU.subtract, ALU.mult))
                P.wait("scalar", "V", vb2)
                atm = a_do(lambda s, xb=xb: s.activation(htmp[:, 0:D], xb, AF.Identity, bias=stat[:, 7:8], scale=stat[:, 6:7]))
                P.wait("vector", "A", atm)
                vlo = v_do(lambda v: v.tensor_sub(hlob[:], htmp[:, 0:D], hhib[:]))
                for c4 in range(ceil_div(C, 4) if cfg.get("stopA", 3) >= -1 else 0):
                    n4 = min(4, C - c4 * 4)
                    P.wait("tensor", "V", vlo)
                    P.wait("tensor", "A", P.cnt["A"])
                    P.wait("tensor", "V", P.cnt["V"])
                    for k in range(n4):
                        cc = c4 * 4 + k
                        t_do(lambda t, cc=cc, k=k: t.transpose(
                            pT1[:, k * 128:(k + 1) * 128], hhib[:, cc * 128:(cc + 1) * 128], identb[:]))
                    tl = None
                    for k in range(n4):
                        cc = c4 * 4 + k
                        tl = t_do(lambda t, cc=cc, k=k: t.transpose(
                            pT2[:, k * 128:(k + 1) * 128], hlob[:, cc * 128:(cc + 1) * 128], identb[:]))
                    P.wait("vector", "T", tl)
                    for k in range(n4):
                        cc = c4 * 4 + k
                        v_do(lambda v, cc=cc, k=k, tt=tt: v.tensor_copy(
                            hT["hi"][:, cc * S + tt * 128: cc * S + (tt + 1) * 128], pT1[:, k * 128:(k + 1) * 128]))
                    for k in range(n4):
                        cc = c4 * 4 + k
                        v_do(lambda v, cc=cc, k=k, tt=tt: v.tensor_copy(
                            hT["lo"][:, cc * S + tt * 128: cc * S + (tt + 1) * 128], pT2[:, k * 128:(k + 1) * 128]))
            hT_V, hT_A = P.cnt["V"], P.cnt["A"]
            if cfg.get("stopA", 3) < 0:
                pass
            if "hT" in dbg:
                P.barrier()
                for p in ("hi", "lo"):
                    for c in range(C):
                        vv = v_do(lambda v, p=p, c=c: v.tensor_copy(htmp[:, 0:S], hT[p][:, c * S:(c + 1) * S]))
                        P.wait("gpsimd", "V", vv)
                        sdd = st(dbg_t[f"hT_{p}"].ap()[c * 128:(c + 1) * 128, :], htmp[:, 0:S])
                        P.wait("vector", "dS", sdd)
                P.barrier()

            _wlist = ("wq", "wk", "wv")[:cfg.get("stopA", 3)]
            for nm in _wlist:
                for h in range(C):
                    wb = whead[h % 2]
                    P.wait("sync", "T", P.cnt["T"])
                    load_wtile(wb["hi"], nm, "hi", h)
                    dl = load_wtile(wb["lo"], nm, "lo", h)
                    P.wait("tensor", "dL", dl)
                    P.wait("tensor", "A", P.cnt["A"])
                    P.wait("tensor", "V", P.cnt["V"])
                    tmm = None
                    npass = 0
                    for c in range(C):
                        for (wp, hp) in (("hi", "hi"), ("hi", "lo"), ("lo", "hi")):
                            first = npass == 0
                            last = npass == 3 * C - 1
                            npass += 1
                            for sl in range(NSL):
                                tmm = t_do(lambda t, wb=wb, wp=wp, hp=hp, c=c, sl=sl, first=first, last=last: t.matmul(
                                    pQ[sl][:, 0:SLW],
                                    wb[wp][:, c * 128:(c + 1) * 128],
                                    hT[hp][:, c * S + sl * SLW: c * S + (sl + 1) * SLW],
                                    start=first, stop=last))
                    for sl in range(NSL):
                        ds0 = P.cnt["dS"]
                        P.wait("scalar", "T", tmm)
                        P.wait("scalar", "dS", ds0)
                        ah = a_do(lambda s, sl=sl: s.copy(evhi[:, 0:SLW], pQ[sl][:, 0:SLW]))
                        P.wait("vector", "T", tmm)
                        P.wait("vector", "A", ah)
                        P.wait("vector", "dS", ds0)
                        vl = v_do(lambda v, sl=sl: v.tensor_sub(evlo[:, 0:SLW], pQ[sl][:, 0:SLW], evhi[:, 0:SLW]))
                        if nm == "wv":
                            P.wait("tensor", "A", ah)
                            P.wait("tensor", "V", vl)
                            P.wait("tensor", "A", P.cnt["A"])
                            P.wait("tensor", "V", P.cnt["V"])
                            tl = None
                            for k4 in range(SLW // 128):
                                t_do(lambda t, k4=k4: t.transpose(
                                    pT1[:, k4 * 128:(k4 + 1) * 128], evhi[:, k4 * 128:(k4 + 1) * 128], identb[:]))
                                tl = t_do(lambda t, k4=k4: t.transpose(
                                    pT2[:, k4 * 128:(k4 + 1) * 128], evlo[:, k4 * 128:(k4 + 1) * 128], identb[:]))
                            ds0 = P.cnt["dS"]
                            P.wait("vector", "T", tl)
                            P.wait("vector", "dS", ds0)
                            ah2 = v_do(lambda v: v.tensor_copy(vtr["hi"][:, 0:SLW], pT1[:, 0:SLW]))
                            vl2 = v_do(lambda v: v.tensor_copy(vtr["lo"][:, 0:SLW], pT2[:, 0:SLW]))
                            P.wait("gpsimd", "V", vl2)
                            for k4 in range(SLW // 128):
                                tok0 = sl * SLW + k4 * 128
                                st(vs["hi"].ap()[h, tok0:tok0 + 128, :], vtr["hi"][:, k4 * 128:(k4 + 1) * 128])
                                st(vs["lo"].ap()[h, tok0:tok0 + 128, :], vtr["lo"][:, k4 * 128:(k4 + 1) * 128])
                        else:
                            dst = qs if nm == "wq" else ks
                            P.wait("gpsimd", "A", ah)
                            P.wait("gpsimd", "V", vl)
                            st(dst["hi"].ap()[h * 128:(h + 1) * 128, sl * SLW:(sl + 1) * SLW], evhi[:, 0:SLW])
                            st(dst["lo"].ap()[h * 128:(h + 1) * 128, sl * SLW:(sl + 1) * SLW], evlo[:, 0:SLW])
            if "qkv" in dbg:
                P.barrier()
                for nm2, srcd in (("q", qs), ("k", ks)):
                    for c in range(C):
                        for s0 in range(0, S, 512):
                            dda = ld(evhi[:, 0:512], srcd["hi"].ap()[c * 128:(c + 1) * 128, s0:s0 + 512])
                            ddb = ld(evlo[:, 0:512], srcd["lo"].ap()[c * 128:(c + 1) * 128, s0:s0 + 512])
                            P.wait("vector", "dL", ddb)
                            vv = v_do(lambda v: v.tensor_add(htmp[:, 0:512], evhi[:, 0:512], evlo[:, 0:512]))
                            P.wait("gpsimd", "V", vv)
                            sdd = st(dbg_t[nm2].ap()[c * 128:(c + 1) * 128, s0:s0 + 512], htmp[:, 0:512])
                            P.wait("sync", "dS", sdd)
                for h in range(H):
                    for tt2 in range(TT):
                        dda = ld(evhi[:, 0:HD], vs["hi"].ap()[h, tt2 * 128:(tt2 + 1) * 128, :])
                        ddb = ld(evlo[:, 0:HD], vs["lo"].ap()[h, tt2 * 128:(tt2 + 1) * 128, :])
                        P.wait("vector", "dL", ddb)
                        vv = v_do(lambda v: v.tensor_add(htmp[:, 0:HD], evhi[:, 0:HD], evlo[:, 0:HD]))
                        P.wait("gpsimd", "V", vv)
                        sdd = st(dbg_t["v"].ap()[h, tt2 * 128:(tt2 + 1) * 128, :], htmp[:, 0:HD])
                        P.wait("sync", "dS", sdd)
                P.barrier()
        P.barrier()

        if STOP >= 2:
            khb = [{"hi": kq4[0], "lo": kq4[1]} for i in range(2)]
            qhb = [{"hi": kq4[2], "lo": kq4[3]} for i in range(2)]
            vhb = [{"hi": vh2[0], "lo": vh2[1]} for i in range(2)]
            expt = pool[0][:, 0:QC]
            pbuf = [{"hi": ev4[0][:, 0:QC], "lo": ev4[1][:, 0:QC]} for i in range(2)]
            cscrW = pool[1][:, 0:QC]
            chibW = ev4[2][:, 0:QC]
            clobW = ev4[3][:, 0:QC]
            dbgtmp = pool[2][:, 0:S] if "ctx" in dbg else None

            for h in range(H):
                kb, qb, vb = khb[h % 2], qhb[h % 2], vhb[h % 2]
                P.wait("sync", "T", P.cnt["T"])
                ld(kb["hi"][:, 0:S], ks["hi"].ap()[h * 128:(h + 1) * 128, :])
                ld(kb["lo"][:, 0:S], ks["lo"].ap()[h * 128:(h + 1) * 128, :])
                ld(qb["hi"][:, 0:S], qs["hi"].ap()[h * 128:(h + 1) * 128, :])
                dq = ld(qb["lo"][:, 0:S], qs["lo"].ap()[h * 128:(h + 1) * 128, :])
                P.wait("gpsimd", "T", P.cnt["T"])
                dv = None
                for p, fill in (("hi", 1.0), ("lo", 0.0)):
                    for j4 in range(ceil_div(TT, 4)):
                        nj = min(4, TT - j4 * 4)
                        dv = P.dma("gpsimd",
                                   bass.AP(vb[p], j4 * 4 * 129, [[TT * 129, 128], [129, nj], [1, 128]]),
                                   vs[p].ap()[h, j4 * 4 * 128:(j4 * 4 + nj) * 128, :].rearrange("(j p) n -> p j n", p=128),
                                   "dS")
                    P.wait("vector", "dS", dv)
                    P.wait("vector", "T", P.cnt["T"])
                    v_do(lambda v, p=p, fill=fill: v.memset(
                        bass.AP(vb[p], 128, [[TT * 129, 128], [129, TT], [1, 1]]), fill))
                vmset = P.cnt["V"]
                stopB = cfg.get("stopB", 9)
                for qc in range(NQC if stopB >= 2 else 0):
                    started_banks = set()
                    for jt in range((qc + 1) * JPQ):
                        diag = jt - qc * JPQ
                        col0 = max(0, jt * 128 - qc * QC)
                        ncol = QC - col0
                        P.wait("tensor", "dL", dq)
                        P.wait("tensor", "dS", dv)
                        P.wait("tensor", "V", vmset)
                        P.wait("tensor", "A", P.cnt["A"])
                        P.wait("tensor", "V", P.cnt["V"])
                        mm = None
                        for (kp, qp) in (("hi", "hi"), ("hi", "lo"), ("lo", "hi")):
                            mm = t_do(lambda t, kp=kp, qp=qp, jt=jt, qc=qc, col0=col0, ncol=ncol, kb=kb, qb=qb: t.matmul(
                                pSC[:, col0:col0 + ncol],
                                kb[kp][:, jt * 128:(jt + 1) * 128],
                                qb[qp][:, qc * QC + col0: qc * QC + col0 + ncol],
                                start=(kp == "hi" and qp == "hi"), stop=(kp == "lo")))
                        if diag >= 0:
                            P.wait("vector", "T", mm)
                            mmv = v_do(lambda v, col0=col0: v.tensor_add(
                                pSC[:, col0:col0 + 128], pSC[:, col0:col0 + 128], cmask[:]))
                            P.wait("scalar", "V", mmv)
                        P.wait("scalar", "T", mm)
                        P.wait("scalar", "V", P.cnt["V"])
                        ae = a_do(lambda s, col0=col0, ncol=ncol: s.activation(
                            expt[:, col0:col0 + ncol], pSC[:, col0:col0 + ncol], AF.Exp, bias=negC[:], scale=SC_SOFT))
                        pb = pbuf[jt % 2]
                        if stopB < 3:
                            continue
                        P.wait("vector", "A", ae)
                        P.wait("vector", "T", P.cnt["T"])
                        v_do(lambda v, pb=pb, col0=col0, ncol=ncol: v.tensor_copy(
                            pb["hi"][:, col0:col0 + ncol], expt[:, col0:col0 + ncol]))
                        vl_ = v_do(lambda v, pb=pb, col0=col0, ncol=ncol: v.tensor_sub(
                            pb["lo"][:, col0:col0 + ncol], expt[:, col0:col0 + ncol], pb["hi"][:, col0:col0 + ncol]))
                        for t_ in range(JPQ if stopB >= 4 else 0):
                            qt = qc * JPQ + t_
                            if jt > qt:
                                continue
                            pcx = pCX[t_]
                            bk = t_ // 2
                            start_bank = bk not in started_banks
                            t_top = min(bk * 2 + 1, JPQ - 1)
                            last = (t_ == t_top) and (jt == qt)
                            P.wait("tensor", "V", vl_)
                            if start_bank:
                                started_banks.add(bk)
                                P.wait("tensor", "A", P.cnt["A"])
                                P.wait("tensor", "V", P.cnt["V"])
                            scol = t_ * 128
                            for ip, (pp, vp) in enumerate((("hi", "hi"), ("hi", "lo"), ("lo", "hi"))):
                                t_do(lambda t, pb=pb, pp=pp, vp=vp, jt=jt, scol=scol, pcx=pcx, vb=vb,
                                     sb_=(start_bank and ip == 0), lb_=(last and ip == 2): t.matmul(
                                    pcx,
                                    pb[pp][:, scol:scol + 128],
                                    vb[vp][:, jt * 129:(jt + 1) * 129],
                                    start=sb_, stop=lb_))
                    avdone = P.cnt["T"]
                    if stopB >= 5:
                        for t_ in range(JPQ):
                            pcx = pCX[t_]
                            P.wait("vector", "T", avdone)
                            v_do(lambda v, pcx=pcx, t_=t_: v.reciprocal(lrec[:, t_:t_ + 1], pcx[:, 128:129]))
                        P.wait("scalar", "T", avdone)
                        P.wait("scalar", "T", P.cnt["T"])
                        P.wait("scalar", "V", P.cnt["V"])
                        ah = None
                        for t_ in range(JPQ):
                            pcx = pCX[t_]
                            ah = a_do(lambda s, pcx=pcx, t_=t_: s.activation(
                                chibW[:, t_ * 128:(t_ + 1) * 128], pcx[:, 0:128], AF.Identity,
                                bias=0.0, scale=lrec[:, t_:t_ + 1]))
                        for t_ in range(JPQ):
                            pcx = pCX[t_]
                            v_do(lambda v, pcx=pcx, t_=t_: v.tensor_scalar_mul(
                                cscrW[:, t_ * 128:(t_ + 1) * 128], pcx[:, 0:128], lrec[:, t_:t_ + 1]))
                        P.wait("vector", "A", ah)
                        vlo2 = v_do(lambda v: v.tensor_sub(clobW[:, 0:QC], cscrW[:, 0:QC], chibW[:, 0:QC]))
                        P.wait("tensor", "A", ah)
                        P.wait("tensor", "V", vlo2)
                        P.wait("tensor", "A", P.cnt["A"])
                        P.wait("tensor", "V", P.cnt["V"])
                        P.sem("TF")
                        for t_ in range(JPQ):
                            t_do(lambda t, t_=t_: t.transpose(
                                pT1[:, t_ * 128:(t_ + 1) * 128], chibW[:, t_ * 128:(t_ + 1) * 128], identb[:]))
                        tfv = None
                        for t_ in range(JPQ):
                            fn = lambda t, t_=t_: t.transpose(
                                pT2[:, t_ * 128:(t_ + 1) * 128], clobW[:, t_ * 128:(t_ + 1) * 128], identb[:])
                            if t_ == JPQ - 1:
                                tfv = P.do("tensor", fn, inc="TF")
                            else:
                                t_do(fn)
                        if stopB >= 9:
                            P.wait("vector", "TF", tfv)
                            v_do(lambda v, h=h, qc=qc: v.tensor_copy(
                                cxT["hi"][:, h * S + qc * QC: h * S + (qc + 1) * QC], pT1[:, 0:QC]))
                            v_do(lambda v, h=h, qc=qc: v.tensor_copy(
                                cxT["lo"][:, h * S + qc * QC: h * S + (qc + 1) * QC], pT2[:, 0:QC]))
            if "ctx" in dbg:
                P.barrier()
                for c in range(C):
                    v_do(lambda v, c=c: v.tensor_copy(dbgtmp, cxT["hi"][:, c * S:(c + 1) * S]))
                    vv = v_do(lambda v, c=c: v.tensor_tensor(dbgtmp, dbgtmp, cxT["lo"][:, c * S:(c + 1) * S], ALU.add))
                    P.wait("gpsimd", "V", vv)
                    sdd = st(dbg_t["ctx"].ap()[c * 128:(c + 1) * 128, :], dbgtmp)
                    P.wait("vector", "dS", sdd)
                P.barrier()
        P.barrier()

        if STOP >= 3:
            wotile = [{"hi": wtile[0], "lo": wtile[1]} for i in range(2)]
            oTc = pool[0][:, 0:S]
            xcol = [pool[1][:, 0:S], pool[1][:, 0:S]]
            x2b = pool[2][:, 0:S]

            for dt_ in range(C):
                wb = wotile[dt_ % 2]
                P.wait("sync", "T", P.cnt["T"])
                load_wtile(wb["hi"], "wo", "hi", dt_)
                dwl = load_wtile(wb["lo"], "wo", "lo", dt_)
                xc = xcol[dt_ % 2]
                P.wait("sync", "V", P.cnt["V"])
                dxc = None
                for j4 in range(ceil_div(TT, 4)):
                    nj = min(4, TT - j4 * 4)
                    dxc = ld(xc.rearrange("p (j n) -> p j n", n=128)[:, j4 * 4:j4 * 4 + nj, :],
                             x_in.ap()[b, :, dt_ * 128:(dt_ + 1) * 128].rearrange("(j p) n -> p j n", p=128)[:, j4 * 4:j4 * 4 + nj, :])
                P.wait("tensor", "dL", dwl)
                P.wait("tensor", "A", P.cnt["A"])
                P.wait("tensor", "V", P.cnt["V"])
                tmm = None
                npass = 0
                for c in range(C):
                    for (wp, cp) in (("hi", "hi"), ("hi", "lo"), ("lo", "hi")):
                        first = npass == 0
                        last = npass == 3 * C - 1
                        npass += 1
                        for sl in range(NSL):
                            tmm = t_do(lambda t, wb=wb, wp=wp, cp=cp, c=c, sl=sl, first=first, last=last: t.matmul(
                                pQ[sl][:, 0:SLW],
                                wb[wp][:, c * 128:(c + 1) * 128],
                                cxT[cp][:, c * S + sl * SLW: c * S + (sl + 1) * SLW],
                                start=first, stop=last))
                P.wait("scalar", "T", tmm)
                ae = None
                for sl in range(NSL):
                    ae = a_do(lambda s, sl=sl, dt_=dt_, oTc=oTc: s.activation(
                        oTc[:, sl * SLW:(sl + 1) * SLW], pQ[sl][:, 0:SLW], AF.Identity,
                        bias=bo_sb[:, dt_:dt_ + 1], scale=1.0))
                for g4 in range(TT // 4):
                    P.wait("tensor", "A", ae)
                    P.wait("tensor", "V", P.cnt["V"])
                    tt_ = None
                    for k in range(4):
                        tt_ = t_do(lambda t, g4=g4, k=k, oTc=oTc: t.transpose(
                            pTf[:, k * 128:(k + 1) * 128], oTc[:, (g4 * 4 + k) * 128:(g4 * 4 + k + 1) * 128], identf[:]))
                    P.wait("vector", "T", tt_)
                    P.wait("vector", "dL", dxc)
                    P.wait("vector", "dS", P.cnt["dS"])
                    v_do(lambda v, g4=g4, x2b=x2b, xc=xc: v.tensor_add(
                        x2b[:, g4 * 512:(g4 + 1) * 512], pTf[:, 0:512], xc[:, g4 * 512:(g4 + 1) * 512]))
                P.wait("gpsimd", "V", P.cnt["V"])
                for j4 in range(ceil_div(TT, 4)):
                    nj = min(4, TT - j4 * 4)
                    st(x2s.ap()[:, dt_ * 128:(dt_ + 1) * 128].rearrange("(j p) n -> p j n", p=128)[:, j4 * 4:j4 * 4 + nj, :],
                       x2b.rearrange("p (j n) -> p j n", n=128)[:, j4 * 4:j4 * 4 + nj, :])
            if "x2" in dbg:
                P.barrier()
                for tt2 in range(TT):
                    ddy = ld(pool[1][:, 0:D], x2s.ap()[tt2 * 128:(tt2 + 1) * 128, :])
                    P.wait("gpsimd", "dL", ddy)
                    sdd = st(dbg_t["x2"].ap()[tt2 * 128:(tt2 + 1) * 128, :], pool[1][:, 0:D])
                    P.wait("sync", "dS", sdd)
                P.barrier()
        P.barrier()

        if STOP >= 4:
            x2t = [pool[0][:, 0:D], pool[0][:, 0:D]]
            sqt = pool[1][:, 0:D]
            qg1 = pool[1][:, 0:D]
            qg2 = pool[2][:, 0:D]
            xqb = bpool[0]
            wfsb = [{"hi": wtile[0], "lo": wtile[1]} for i in range(2)]
            oTc = pool[3][:, 0:S]
            xcol = [pool[0][:, 0:S], pool[0][:, 0:S]]
            x2b = pool[2][:, 0:S]

            for tt in range(TT):
                xb = x2t[tt % 2]
                P.wait("sync", "A", P.cnt["A"])
                P.wait("sync", "V", P.cnt["V"])
                dx = ld(xb[:], x2s.ap()[tt * 128:(tt + 1) * 128, :])
                P.wait("vector", "dL", dx)
                v_do(lambda v, xb=xb: v.reduce_sum(stat[:, 0:1], xb, axis=AX.X))
                P.wait("scalar", "dL", dx)
                sq = a_do(lambda s, xb=xb: s.activation(sqt, xb, AF.Square, accum_out=stat[:, 1:2]))
                v_do(lambda v: v.tensor_scalar_mul(stat[:, 2:3], stat[:, 0:1], 1.0 / D))
                P.wait("vector", "A", sq)
                v_do(lambda v: v.tensor_mul(stat[:, 3:4], stat[:, 2:3], stat[:, 2:3]))
                v_do(lambda v: v.tensor_scalar(stat[:, 4:5], stat[:, 1:2], 1.0 / D, None, ALU.mult))
                v_do(lambda v: v.tensor_sub(stat[:, 4:5], stat[:, 4:5], stat[:, 3:4]))
                vv = v_do(lambda v: v.tensor_scalar_add(stat[:, 4:5], stat[:, 4:5], EPS))
                P.wait("scalar", "V", vv)
                ar = a_do(lambda s: s.activation(stat[:, 5:6], stat[:, 4:5], AF.Sqrt))
                P.wait("vector", "A", ar)
                v_do(lambda v: v.scalar_tensor_tensor(stat[:, 6:7], stat[:, 5:6], TH_X, stat[:, 2:3], ALU.mult, ALU.add))
                v_do(lambda v: v.scalar_tensor_tensor(stat[:, 7:8], stat[:, 5:6], -TH_X, stat[:, 2:3], ALU.mult, ALU.add))
                v_do(lambda v, xb=xb: v.tensor_scalar(qg1, xb, stat[:, 6:7], None, ALU.is_gt))
                v_do(lambda v, xb=xb: v.tensor_scalar(qg2, xb, stat[:, 7:8], None, ALU.is_lt))
                v_do(lambda v: v.tensor_sub(qg1, qg1, qg2))
                vq = v_do(lambda v: v.tensor_scalar_mul(xqb[:], qg1, SC_X))
                for c4 in range(ceil_div(C, 4)):
                    n4 = min(4, C - c4 * 4)
                    P.wait("tensor", "V", vq)
                    P.wait("tensor", "V", P.cnt["V"])
                    th = None
                    for k in range(n4):
                        cc = c4 * 4 + k
                        th = t_do(lambda t, cc=cc, k=k: t.transpose(
                            pT1[:, k * 128:(k + 1) * 128], xqb[:, cc * 128:(cc + 1) * 128], identb[:]))
                    P.wait("vector", "T", th)
                    for k in range(n4):
                        cc = c4 * 4 + k
                        v_do(lambda v, cc=cc, k=k, tt=tt: v.tensor_copy(
                            xqT[:, cc * S + tt * 128: cc * S + (tt + 1) * 128], pT1[:, k * 128:(k + 1) * 128]))
            xq_V = P.cnt["V"]
            for ot in range(C):
                wb = wfsb[ot % 2]
                P.wait("sync", "T", P.cnt["T"])
                load_wtile(wb["hi"], "wfT", "hi", ot)
                dwl = load_wtile(wb["lo"], "wfT", "lo", ot)
                xc = xcol[ot % 2]
                P.wait("sync", "V", P.cnt["V"])
                dxc = None
                for j4 in range(ceil_div(TT, 4)):
                    nj = min(4, TT - j4 * 4)
                    dxc = ld(xc.rearrange("p (j n) -> p j n", n=128)[:, j4 * 4:j4 * 4 + nj, :],
                             x2s.ap()[:, ot * 128:(ot + 1) * 128].rearrange("(j p) n -> p j n", p=128)[:, j4 * 4:j4 * 4 + nj, :])
                P.wait("tensor", "dL", dwl)
                P.wait("tensor", "V", xq_V)
                P.wait("tensor", "A", P.cnt["A"])
                P.wait("tensor", "V", P.cnt["V"])
                tmm = None
                npass = 0
                for c in range(C):
                    for wp in ("hi", "lo"):
                        first = npass == 0
                        last = npass == 2 * C - 1
                        npass += 1
                        for sl in range(NSL):
                            tmm = t_do(lambda t, wb=wb, wp=wp, c=c, sl=sl, first=first, last=last: t.matmul(
                                pQ[sl][:, 0:SLW],
                                wb[wp][:, c * 128:(c + 1) * 128],
                                xqT[:, c * S + sl * SLW: c * S + (sl + 1) * SLW],
                                start=first, stop=last))
                P.wait("scalar", "T", tmm)
                ae = None
                for sl in range(NSL):
                    ae = a_do(lambda s, sl=sl, oTc=oTc: s.copy(oTc[:, sl * SLW:(sl + 1) * SLW], pQ[sl][:, 0:SLW]))
                for g4 in range(TT // 4):
                    P.wait("tensor", "A", ae)
                    P.wait("tensor", "V", P.cnt["V"])
                    tt_ = None
                    for k in range(4):
                        tt_ = t_do(lambda t, g4=g4, k=k, oTc=oTc: t.transpose(
                            pTf[:, k * 128:(k + 1) * 128], oTc[:, (g4 * 4 + k) * 128:(g4 * 4 + k + 1) * 128], identf[:]))
                    P.wait("vector", "T", tt_)
                    P.wait("vector", "dL", dxc)
                    P.wait("vector", "dS", P.cnt["dS"])
                    v_do(lambda v, g4=g4, x2b=x2b, xc=xc: v.tensor_add(
                        x2b[:, g4 * 512:(g4 + 1) * 512], pTf[:, 0:512], xc[:, g4 * 512:(g4 + 1) * 512]))
                P.wait("gpsimd", "V", P.cnt["V"])
                for j4 in range(ceil_div(TT, 4)):
                    nj = min(4, TT - j4 * 4)
                    st(y_out.ap()[b, :, ot * 128:(ot + 1) * 128].rearrange("(j p) n -> p j n", p=128)[:, j4 * 4:j4 * 4 + nj, :],
                       x2b.rearrange("p (j n) -> p j n", n=128)[:, j4 * 4:j4 * 4 + nj, :])
        P.barrier()

    P.wait("sync", "dS", P.cnt["dS"])
    P.wait("sync", "dL", P.cnt["dL"])

    with nc.Block() as block:
        P.replay(block)
    nc._keepalive_stack = stack
    return nc


from concourse.bass_utils import run_bass_kernel_spmd

_CFG = {"S": 2048, "D": 2048, "H": 16, "B": 2}
_NCORES = 8


def _consts():
    ident = np.eye(128, dtype=np.float32)
    cm = np.zeros((128, 128), np.float32)
    for j in range(128):
        cm[j, :j] = -1e30
    return ident, cm


def kernel(L0_attn_x, L0_attn_Wq, L0_attn_Wk, L0_attn_Wv, L0_attn_Wo, L0_attn_bo,
           L0_ffn_weight, L0_ffn_rram_variation):
    D = _CFG["D"]
    x = np.ascontiguousarray(np.asarray(L0_attn_x, dtype=np.float32))
    wq = np.ascontiguousarray(np.asarray(L0_attn_Wq, dtype=np.float32))
    wk = np.ascontiguousarray(np.asarray(L0_attn_Wk, dtype=np.float32))
    wv = np.ascontiguousarray(np.asarray(L0_attn_Wv, dtype=np.float32))
    wo = np.ascontiguousarray(np.asarray(L0_attn_Wo, dtype=np.float32))
    bo = np.ascontiguousarray(np.asarray(L0_attn_bo, dtype=np.float32)).reshape(1, D)
    wf = np.ascontiguousarray(np.asarray(L0_ffn_weight, dtype=np.float32))
    rr = np.ascontiguousarray(np.asarray(L0_ffn_rram_variation, dtype=np.float32))
    ident, cm = _consts()

    nc = build_program(_CFG)
    bpc = x.shape[0] // _NCORES
    in_maps = [{
        "x": x[c * bpc:(c + 1) * bpc],
        "wq": wq, "wk": wk, "wv": wv, "wo": wo, "bo": bo,
        "wf": wf, "rr": rr, "identf": ident, "cmask": cm,
    } for c in range(_NCORES)]
    res = run_bass_kernel_spmd(nc, in_maps, core_ids=list(range(_NCORES)))
    return np.concatenate([np.asarray(r["y"], dtype=np.float32) for r in res.results], axis=0)
